# version 01 (final)
# baseline (speedup 1.0000x reference)
import os
import contextlib
import numpy as np
import ml_dtypes
import concourse.bass as bass
import concourse.mybir as mybir
from concourse.bass_utils import run_bass_kernel_spmd

F32 = mybir.dt.float32
BF16 = mybir.dt.bfloat16
I32 = mybir.dt.int32
AF = mybir.ActivationFunctionType
ALU = mybir.AluOpType
AX = mybir.AxisListType.X

B, S, D, H, DH, E, FF = 2, 8192, 1024, 16, 64, 8, 4096
RPC = 2048
H4 = 256
LR = 0.001
EPS = 1e-5
L2EPS = 1e-12
NDIV = 2.0 / (S * D)

MAXPH = int(os.environ.get("MK_MAXPH", "99"))
PROBE = os.environ.get("MK_PROBE", "")
DBG = set(filter(None, os.environ.get("MK_DBG", "").split(",")))


def build_program():
    nc = bass.Bass("TRN2", target_bir_lowering=False, debug=False)

    def din(name, shape, dt=F32):
        return nc.dram_tensor(name, shape, dt, kind="ExternalInput").ap()

    x_full = din("x_full", [S, D])
    x_own = din("x_own", [RPC, D])
    tkT = din("tkT", [D, D])
    tvT = din("tvT", [D, D])
    tqT = din("tqT", [D, D])
    W1T_i = din("W1T", [D, H4])
    W2T_i = din("W2T", [H4, D])
    W2n_i = din("W2n", [D, H4])
    b1c_i = din("b1c", [128, 2])
    b2c_i = din("b2c", [128, 8])
    lng_i = din("lng", [128, 8])
    lnb_i = din("lnb", [128, 8])
    Wq_i = din("Wq", [D, 256])
    Wk_i = din("Wk", [D, 256])
    Wv_i = din("Wv", [D, 256])
    qs_i = din("qs", [128, 2])
    ks_i = din("ks", [128, 2])
    Wo_i = din("Wo", [D, D])
    gw_i = din("gw", [D, E])
    gb_i = din("gb", [E, 1])
    w1_i = din("w1", [E, D, FF])
    w2_i = din("w2", [E, FF, D])
    b1m_i = din("b1m", [128, E * 32])
    b2m_i = din("b2m", [E, D])
    msk_i = din("msk", [128, 2])
    roff_i = din("roff", [1, 1], I32)
    ident_f = din("ident_f", [128, 128])
    ident_b = din("ident_b", [128, 128], BF16)
    ones1_128f = din("ones1_128f", [1, 128])
    ones_col_f = din("ones_col_f", [128, 1])
    ones1_64b = din("ones1_64b", [1, 64], BF16)
    ones2w_b = din("ones2w_b", [128, 65], BF16)
    ones8f = din("ones8f", [8, 1])
    onehot_f = din("onehot_f", [8, E * 128])
    cmask_i = din("cmask_in", [128, 4 * 512], BF16)

    out_o = nc.dram_tensor("out", [RPC, D], F32, kind="ExternalOutput").ap()

    dbg_outs = {}

    def dscr(name, shape, dt):
        if name.split("_")[0] in DBG:
            dbg_outs[name] = nc.dram_tensor(name + "bg", shape, dt,
                                            kind="ExternalOutput").ap()
        return nc.dram_tensor(name, shape, dt).ap()

    ttt_d = dscr("ttt_d", [RPC, D], F32)
    agi_d = dscr("agi_d", [256, S], BF16)
    ago_d = dscr("ago_d", [1024, S], BF16)
    g1i_d = dscr("g1i_d", [514, 1024], F32)
    g1o_d = dscr("g1o_d", [514, 1024], F32)
    g2i_d = dscr("g2i_d", [514, 1024], F32)
    g2o_d = dscr("g2o_d", [514, 1024], F32)
    h2_d = dscr("h2_d", [D, RPC], BF16)
    qs_dbg = dscr("qsd", [128, 2 * S], BF16) if "qsd" in DBG else None

    cnt = {}

    class SemBox:
        def __init__(self, handle, name):
            self.h = handle
            self.name = name
            cnt[name] = 0

        def inc(self, instr, k=1):
            cnt[self.name] += k
            instr.then_inc(self.h, k)
            return cnt[self.name]

        @property
        def v(self):
            return cnt[self.name]

    ES = contextlib.ExitStack()
    with ES:
        def sb(name, shape, dt, stack=None):
            return (stack or ES).enter_context(nc.sbuf_tensor(name, shape, dt))

        def mksem(name):
            return SemBox(ES.enter_context(nc.semaphore(name)), name)

        CH = mksem("ch")
        LD = mksem("ld")
        LW = mksem("lw")
        CC = mksem("cc")
        SM = mksem("sm")
        SX = mksem("sx")
        SXA = mksem("sxa")
        SPV = mksem("spv")
        SEV = mksem("sev")
        MH = mksem("mh")
        ME = mksem("me")
        MX = mksem("mx")
        MV = mksem("mv")
        MA = mksem("ma")
        SO = mksem("so")

        PB = [ES.enter_context(nc.psum_tensor("pb%d" % i, [128, 512], F32))
              for i in range(7)]
        PT = ES.enter_context(nc.psum_tensor("ptb", [128, 1024], BF16))

        identf = sb("identf", [128, 128], F32)
        identb = sb("identb", [128, 128], BF16)
        on1_128f = sb("on1_128f", [1, 128], F32)
        onc_f = sb("onc_f", [128, 1], F32)
        on1_64b = sb("on1_64b", [1, 64], BF16)
        on2w_b = sb("on2w_b", [128, 65], BF16)
        on8f = sb("on8f", [8, 1], F32)
        onehotf = sb("onehotf", [8, E * 128], F32)
        cmask = sb("cmask", [128, 4 * 512], BF16)
        lngc = sb("lngc", [128, 8], F32)
        lnbc = sb("lnbc", [128, 8], F32)
        mskc = sb("mskc", [128, 4], F32)
        rofft = sb("rofft", [1, 1], I32)
        gbcol = sb("gbcol", [8, 1], F32)
        b1mc = sb("b1mc", [128, E * 32], F32)
        b2m_b = sb("b2m_b", [8, D], BF16)
        qscol = sb("qscol", [128, 2], F32)
        kscol = sb("kscol", [128, 2], F32)
        gw_b = sb("gw_b", [128, 8 * 8], BF16)
        rws = sb("rws", [128, 16], F32)
        cst = sb("cst", [128, 4], F32)
        nc.const_aps.register(F32, EPS, cst[:, 0:1])
        nc.const_aps.register(F32, L2EPS, cst[:, 1:2])
        nc.const_aps.register(F32, 10.0, cst[:, 2:3])
        rrow = sb("rrow", [1, 16], F32)
        rr_b = sb("rr_b", [1, 8], BF16)

        es_ttt = contextlib.ExitStack()
        W1T_f = sb("W1T_f", [128, 8 * 256], F32, es_ttt)
        W1T_b = sb("W1T_b", [128, 8 * 256], BF16, es_ttt)
        W2T_f = sb("W2T_f", [128, 2 * 1024], F32, es_ttt)
        W2T_b = sb("W2T_b", [128, 2 * 1024], BF16, es_ttt)
        b1col = sb("b1col", [128, 2], F32, es_ttt)
        b2col = sb("b2col", [128, 16], F32, es_ttt)

        probe_state = {"done": False, "t": None}

        def probe(tag, get_ap, shape, dt):
            if PROBE != tag or probe_state["done"]:
                return
            probe_state["done"] = True
            pt = nc.dram_tensor("probe", shape, dt, kind="ExternalOutput").ap()

            @block.sync
            def _(sy):
                sy.wait_ge(CH.h, CH.v)
                LD.inc(sy.dma_start(out=pt, in_=get_ap()), 16)

        with nc.Block() as block:
            @block.sync
            def _(sy):
                for dst, src in [
                    (identf[:], ident_f), (on1_128f[:], ones1_128f),
                    (onc_f[:], ones_col_f),
                    (on8f[:], ones8f), (onehotf[:], onehot_f),
                    (lngc[:], lng_i), (lnbc[:], lnb_i),
                    (mskc[:, 0:2], msk_i), (qscol[:], qs_i), (kscol[:], ks_i),
                    (gbcol[:], gb_i), (b1mc[:], b1m_i),
                    (identb[:], ident_b), (on1_64b[:], ones1_64b),
                    (on2w_b[:], ones2w_b), (cmask[:], cmask_i),
                    (rofft[:], roff_i), (b1col[:], b1c_i), (b2col[:, 0:8], b2c_i),
                ]:
                    LD.inc(sy.dma_start(out=dst, in_=src), 16)
                for kc in range(8):
                    LD.inc(sy.dma_start(out=W1T_f[:, kc * 256:(kc + 1) * 256],
                                        in_=W1T_i[kc * 128:(kc + 1) * 128, :]), 16)
                for jh in range(2):
                    LD.inc(sy.dma_start(out=W2T_f[:, jh * 1024:(jh + 1) * 1024],
                                        in_=W2T_i[jh * 128:(jh + 1) * 128, :]), 16)

            @block.gpsimd
            def _(g):
                for kc in range(8):
                    LW.inc(g.dma_start(out=gw_b[:, kc * 8:(kc + 1) * 8],
                                       in_=gw_i[kc * 128:(kc + 1) * 128, :]), 16)
                LW.inc(g.dma_start(out=b2m_b[:], in_=b2m_i), 16)

            ld0, lw0 = LD.v, LW.v

            @block.vector
            def _(v):
                v.memset(cst[:, 0:1], EPS)
                v.memset(cst[:, 1:2], L2EPS)
                v.memset(cst[:, 2:3], 10.0)
                v.memset(cst[:, 3:4], 0.0)
                v.wait_ge(LD.h, ld0)
                v.wait_ge(LW.h, lw0)
                v.tensor_copy(W1T_b[:], W1T_f[:])
                v.tensor_copy(W2T_b[:], W2T_f[:])
                v.tensor_scalar(out=b2col[:, 8:16], in0=b2col[:, 0:8],
                                scalar1=NDIV, scalar2=None, op0=ALU.mult)
                CH.inc(v.tensor_scalar(out=mskc[:, 2:3], in0=mskc[:, 1:2],
                                       scalar1=-LR, scalar2=None, op0=ALU.mult))

            ch_ph0 = CH.v

            xn_own = sb("xn_own", [128, 8 * RPC], BF16, es_ttt)
            hg_bf = sb("hg_bf", [128, 2 * 512], BF16, es_ttt)
            es_a0 = contextlib.ExitStack()
            xst = sb("xst", [128, 2 * 1024], F32, es_a0)
            xsq = sb("xsq", [128, 1024], F32, es_a0)
            u_tm = sb("u_tm", [128, 1024], BF16, es_a0)

            if MAXPH >= 1:
                ld_a0 = LD.v
                ch_a0_tile = {}
                for i in range(16):
                    xt = xst[:, (i % 2) * 1024:(i % 2 + 1) * 1024]

                    @block.sync
                    def _(sy, i=i):
                        if i == 0:
                            sy.wait_ge(CH.h, ch_ph0)
                        if i >= 2:
                            sy.wait_ge(CH.h, ch_a0_tile[i - 2])
                        LD.inc(sy.dma_start(out=xst[:, (i % 2) * 1024:(i % 2 + 1) * 1024],
                                            in_=x_own[i * 128:(i + 1) * 128, :]), 16)

                    @block.vector
                    def _(v, i=i, xt=xt):
                        v.wait_ge(LD.h, ld_a0 + 16 * (i + 1))
                        v.wait_ge(CH.h, CH.v)
                        CH.inc(v.reduce_sum(rws[:, 0:1], xt, axis=AX))

                    @block.scalar
                    def _(s, xt=xt):
                        s.wait_ge(CH.h, CH.v)
                        CH.inc(s.activation(out=xsq[:], in_=xt, func=AF.Square))

                    @block.vector
                    def _(v, xt=xt):
                        v.wait_ge(CH.h, CH.v)
                        v.reduce_sum(rws[:, 1:2], xsq[:], axis=AX)
                        CH.inc(v.tensor_scalar(out=rws[:, 2:3], in0=rws[:, 0:1],
                                               scalar1=1.0 / D, scalar2=None,
                                               op0=ALU.mult))

                    @block.scalar
                    def _(s):
                        s.wait_ge(CH.h, CH.v)
                        CH.inc(s.activation(out=rws[:, 3:4], in_=rws[:, 2:3],
                                            func=AF.Square))

                    @block.vector
                    def _(v):
                        v.wait_ge(CH.h, CH.v)
                        CH.inc(v.scalar_tensor_tensor(out=rws[:, 5:6], in0=rws[:, 1:2],
                                                      scalar=1.0 / D, in1=rws[:, 3:4],
                                                      op0=ALU.mult, op1=ALU.subtract))

                    @block.scalar
                    def _(s):
                        s.wait_ge(CH.h, CH.v)
                        CH.inc(s.activation(out=rws[:, 6:7], in_=rws[:, 5:6],
                                            func=AF.Sqrt, bias=EPS))

                    @block.vector
                    def _(v):
                        v.wait_ge(CH.h, CH.v)
                        CH.inc(v.reciprocal(rws[:, 7:8], rws[:, 6:7]))

                    @block.vector
                    def _(v, xt=xt):
                        v.wait_ge(CH.h, CH.v)
                        CH.inc(v.tensor_scalar(out=u_tm[:], in0=xt,
                                               scalar1=rws[:, 2:3], scalar2=rws[:, 7:8],
                                               op0=ALU.subtract, op1=ALU.mult))

                    @block.tensor
                    def _(t):
                        t.wait_ge(CH.h, CH.v)
                        t.wait_ge(LD.h, ld0)
                        for kc in range(8):
                            ins = t.transpose(PT[:, kc * 128:(kc + 1) * 128],
                                              u_tm[:, kc * 128:(kc + 1) * 128],
                                              identb[:])
                        CH.inc(ins)

                    @block.vector
                    def _(v, i=i):
                        v.wait_ge(CH.h, CH.v)
                        for kc in range(8):
                            ins = v.tensor_copy(
                                xn_own[:, kc * RPC + i * 128:kc * RPC + (i + 1) * 128],
                                PT[:, kc * 128:(kc + 1) * 128])
                        CH.inc(ins)

                    ch_a0_tile[i] = CH.v

            probe("xn", lambda: xn_own[:], [128, 8 * RPC], BF16)

            if MAXPH >= 2:
                es_a0.close()
                es_g = contextlib.ExitStack()
                tk_b = sb("tk_b", [128, 8 * 1024], BF16, es_g)
                wd_b = sb("wd_b", [128, 8 * 1024], BF16, es_g)
                W2n_f = sb("W2n_f", [128, 8 * 256], F32, es_g)
                W2n_b = sb("W2n_b", [128, 8 * 256], BF16, es_g)
                z_bf = sb("z_bf", [128, 8 * 512], BF16, es_g)
                z_tm = sb("z_tm", [128, 4 * 1024], BF16, es_g)
                dr_bf = sb("dr_bf", [128, 8 * 512], BF16, es_g)
                dr_tm = sb("dr_tm", [128, 4 * 1024], BF16, es_g)
                gp_bf = sb("gp_bf", [128, 2 * 512], BF16, es_g)
                da_bf = sb("da_bf", [128, 2 * 512], BF16, es_g)
                hg_tm = sb("hg_tm", [128, 4 * 256], BF16, es_g)
                da_tm = sb("da_tm", [128, 4 * 256], BF16, es_g)
                dW1a = sb("dW1a", [128, 8 * 256], F32, es_g)
                dW2a = sb("dW2a", [128, 2 * 1024], F32, es_g)
                dba = sb("dba", [128, 10], F32, es_g)

                lw_t0 = LW.v

                @block.gpsimd
                def _(g):
                    g.wait_ge(CH.h, CH.v)
                    for kc in range(8):
                        LW.inc(g.dma_start(out=tk_b[:, kc * 1024:(kc + 1) * 1024],
                                           in_=tkT[kc * 128:(kc + 1) * 128, :]), 16)
                        LW.inc(g.dma_start(out=wd_b[:, kc * 1024:(kc + 1) * 1024],
                                           in_=tvT[kc * 128:(kc + 1) * 128, :]), 16)

                ld_t0 = LD.v

                @block.sync
                def _(sy):
                    sy.wait_ge(CH.h, CH.v)
                    for kc in range(8):
                        LD.inc(sy.dma_start(out=W2n_f[:, kc * 256:(kc + 1) * 256],
                                            in_=W2n_i[kc * 128:(kc + 1) * 128, :]), 16)

                @block.vector
                def _(v):
                    v.wait_ge(CH.h, CH.v)
                    v.wait_ge(LW.h, LW.v)
                    v.wait_ge(LD.h, LD.v)
                    v.tensor_sub(wd_b[:], tk_b[:], wd_b[:])
                    CH.inc(v.tensor_copy(W2n_b[:], W2n_f[:]))

                def emit_grad_pass(pass_id, W1Tb, W2Tb, W2nb, b1c,
                                   mcol, gbuf_in, gbuf_out):
                    @block.vector
                    def _(v):
                        v.wait_ge(CH.h, CH.v)
                        v.memset(dW1a[:], 0.0)
                        v.memset(dW2a[:], 0.0)
                        CH.inc(v.memset(dba[:], 0.0))

                    for stt_i in range(4):
                        tok0 = stt_i * 512

                        for dc in range(8):
                            @block.tensor
                            def _(t, dc=dc):
                                t.wait_ge(CH.h, CH.v)
                                for kc in range(8):
                                    ins = t.matmul(PB[dc % 2][:],
                                                   tk_b[:, kc * 1024 + dc * 128:kc * 1024 + (dc + 1) * 128],
                                                   xn_own[:, kc * RPC + tok0:kc * RPC + tok0 + 512],
                                                   start=(kc == 0), stop=(kc == 7))
                                CH.inc(ins)

                            @block.scalar
                            def _(s, dc=dc):
                                s.wait_ge(CH.h, CH.v)
                                CH.inc(s.activation(out=z_bf[:, dc * 512:(dc + 1) * 512],
                                                    in_=PB[dc % 2][:], func=AF.Copy))

                        probe("z%d" % (10 * pass_id + stt_i),
                              lambda: z_bf[:], [128, 8 * 512], BF16)

                        for jh in range(2):
                            @block.tensor
                            def _(t, jh=jh):
                                t.wait_ge(CH.h, CH.v)
                                for kc in range(8):
                                    ins = t.matmul(PB[2][:],
                                                   W1Tb[:, kc * 256 + jh * 128:kc * 256 + (jh + 1) * 128],
                                                   z_bf[:, kc * 512:(kc + 1) * 512],
                                                   start=(kc == 0), stop=(kc == 7))
                                CH.inc(ins)

                            @block.scalar
                            def _(s, jh=jh):
                                s.wait_ge(CH.h, CH.v)
                                s.activation(out=hg_bf[:, jh * 512:(jh + 1) * 512],
                                             in_=PB[2][:], func=AF.Gelu,
                                             bias=b1c[:, jh:jh + 1])
                                CH.inc(s.activation(out=gp_bf[:, jh * 512:(jh + 1) * 512],
                                                    in_=PB[2][:], func=AF.Derivative_Gelu,
                                                    bias=b1c[:, jh:jh + 1]))

                        probe("hg%d" % (10 * pass_id + stt_i),
                              lambda: hg_bf[:], [128, 2 * 512], BF16)
                        probe("gp%d" % (10 * pass_id + stt_i),
                              lambda: gp_bf[:], [128, 2 * 512], BF16)

                        for dc in range(8):
                            @block.tensor
                            def _(t, dc=dc):
                                t.wait_ge(CH.h, CH.v)
                                for kc in range(8):
                                    t.matmul(PB[dc % 2][:],
                                             wd_b[:, kc * 1024 + dc * 128:kc * 1024 + (dc + 1) * 128],
                                             xn_own[:, kc * RPC + tok0:kc * RPC + tok0 + 512],
                                             start=(kc == 0), stop=False)
                                for jh in range(2):
                                    ins = t.matmul(PB[dc % 2][:],
                                                   W2Tb[:, jh * 1024 + dc * 128:jh * 1024 + (dc + 1) * 128],
                                                   hg_bf[:, jh * 512:(jh + 1) * 512],
                                                   start=False, stop=(jh == 1))
                                CH.inc(ins)

                            @block.vector
                            def _(v, dc=dc):
                                v.wait_ge(CH.h, CH.v)
                                CH.inc(v.tensor_scalar(out=dr_bf[:, dc * 512:(dc + 1) * 512],
                                                       in0=PB[dc % 2][:], scalar1=NDIV,
                                                       scalar2=b2col[:, 8 + dc:9 + dc],
                                                       op0=ALU.mult, op1=ALU.add))

                        @block.vector
                        def _(v):
                            v.wait_ge(CH.h, CH.v)
                            for dc in range(8):
                                ins = v.reduce_sum(rws[:, dc:dc + 1],
                                                   dr_bf[:, dc * 512:(dc + 1) * 512], axis=AX)
                            CH.inc(ins)

                        @block.vector
                        def _(v):
                            v.wait_ge(CH.h, CH.v)
                            CH.inc(v.tensor_add(dba[:, 0:8], dba[:, 0:8], rws[:, 0:8]))

                        probe("dr%d" % (10 * pass_id + stt_i),
                              lambda: dr_bf[:], [128, 8 * 512], BF16)

                        for jh in range(2):
                            @block.tensor
                            def _(t, jh=jh):
                                t.wait_ge(CH.h, CH.v)
                                for kc in range(8):
                                    ins = t.matmul(PB[2][:],
                                                   W2nb[:, kc * 256 + jh * 128:kc * 256 + (jh + 1) * 128],
                                                   dr_bf[:, kc * 512:(kc + 1) * 512],
                                                   start=(kc == 0), stop=(kc == 7))
                                CH.inc(ins)

                            @block.vector
                            def _(v, jh=jh):
                                v.wait_ge(CH.h, CH.v)
                                v.tensor_mul(da_bf[:, jh * 512:(jh + 1) * 512],
                                             PB[2][:], gp_bf[:, jh * 512:(jh + 1) * 512])
                                CH.inc(v.reduce_sum(rws[:, 8 + jh:9 + jh],
                                                    da_bf[:, jh * 512:(jh + 1) * 512],
                                                    axis=AX))

                            @block.vector
                            def _(v, jh=jh):
                                v.wait_ge(CH.h, CH.v)
                                CH.inc(v.tensor_add(dba[:, 8 + jh:9 + jh],
                                                    dba[:, 8 + jh:9 + jh],
                                                    rws[:, 8 + jh:9 + jh]))

                        for (src, dst, nch) in [(z_bf, z_tm, 8), (dr_bf, dr_tm, 8),
                                                (hg_bf, hg_tm, 2), (da_bf, da_tm, 2)]:
                            for nt in range(4):
                                @block.tensor
                                def _(t, src=src, nch=nch, nt=nt):
                                    t.wait_ge(CH.h, CH.v)
                                    for c in range(nch):
                                        ins = t.transpose(
                                            PT[:, (c % 8) * 128:(c % 8 + 1) * 128],
                                            src[:, c * 512 + nt * 128:c * 512 + (nt + 1) * 128],
                                            identb[:])
                                    CH.inc(ins)

                                @block.vector
                                def _(v, dst=dst, nch=nch, nt=nt):
                                    v.wait_ge(CH.h, CH.v)
                                    w = 1024 if nch == 8 else 256
                                    for c in range(nch):
                                        ins = v.tensor_copy(
                                            dst[:, nt * w + c * 128:nt * w + (c + 1) * 128],
                                            PT[:, (c % 8) * 128:(c % 8 + 1) * 128])
                                    CH.inc(ins)

                        for dc in range(8):
                            @block.tensor
                            def _(t, dc=dc):
                                t.wait_ge(CH.h, CH.v)
                                for nt in range(4):
                                    ins = t.matmul(PB[3][:, 0:256],
                                                   z_tm[:, nt * 1024 + dc * 128:nt * 1024 + (dc + 1) * 128],
                                                   da_tm[:, nt * 256:(nt + 1) * 256],
                                                   start=(nt == 0), stop=(nt == 3))
                                CH.inc(ins)

                            @block.vector
                            def _(v, dc=dc):
                                v.wait_ge(CH.h, CH.v)
                                CH.inc(v.tensor_add(dW1a[:, dc * 256:(dc + 1) * 256],
                                                    dW1a[:, dc * 256:(dc + 1) * 256],
                                                    PB[3][:, 0:256]))

                        for jh in range(2):
                            for nh in range(2):
                                @block.tensor
                                def _(t, jh=jh, nh=nh):
                                    t.wait_ge(CH.h, CH.v)
                                    for nt in range(4):
                                        ins = t.matmul(PB[3][:],
                                                       hg_tm[:, nt * 256 + jh * 128:nt * 256 + (jh + 1) * 128],
                                                       dr_tm[:, nt * 1024 + nh * 512:nt * 1024 + (nh + 1) * 512],
                                                       start=(nt == 0), stop=(nt == 3))
                                    CH.inc(ins)

                                @block.vector
                                def _(v, jh=jh, nh=nh):
                                    v.wait_ge(CH.h, CH.v)
                                    o = jh * 1024 + nh * 512
                                    CH.inc(v.tensor_add(dW2a[:, o:o + 512],
                                                        dW2a[:, o:o + 512], PB[3][:]))

                    probe("dW1a_%d" % pass_id, lambda: dW1a[:], [128, 8 * 256], F32)
                    probe("dW2a_%d" % pass_id, lambda: dW2a[:], [128, 2 * 1024], F32)
                    probe("dba_%d" % pass_id, lambda: dba[:], [128, 10], F32)

                    @block.vector
                    def _(v):
                        v.wait_ge(CH.h, CH.v)
                        v.tensor_scalar_mul(dW1a[:], dW1a[:], mcol)
                        v.tensor_scalar_mul(dW2a[:], dW2a[:], mcol)
                        CH.inc(v.tensor_scalar_mul(dba[:], dba[:], mcol))

                    ch_mask = CH.v
                    ld_g0 = LD.v

                    @block.sync
                    def _(sy):
                        sy.wait_ge(CH.h, ch_mask)
                        for jh in range(2):
                            LD.inc(sy.dma_start(out=gbuf_in[jh * 128:(jh + 1) * 128, :],
                                                in_=dW2a[:, jh * 1024:(jh + 1) * 1024]), 16)
                        for dc in range(8):
                            LD.inc(sy.dma_start(
                                out=gbuf_in[256 + (dc // 4) * 128:256 + (dc // 4 + 1) * 128,
                                            (dc % 4) * 256:(dc % 4 + 1) * 256],
                                in_=dW1a[:, dc * 256:(dc + 1) * 256]), 16)
                        LD.inc(sy.dma_start(
                            out=gbuf_in[512:513, :].rearrange("r (p c) -> p r c", p=128),
                            in_=dba[:, 0:8].rearrange("p (r c) -> p r c", r=1)), 16)
                        LD.inc(sy.dma_start(
                            out=gbuf_in[513:514, 0:256].rearrange("r (p c) -> p r c", p=128),
                            in_=dba[:, 8:10].rearrange("p (r c) -> p r c", r=1)), 16)

                    @block.gpsimd
                    def _(g):
                        g.wait_ge(LD.h, ld_g0 + 16 * 12)
                        CC.inc(g.collective_compute(
                            "AllReduce", ALU.add,
                            replica_groups=[[0, 1, 2, 3, 4, 5, 6, 7]],
                            ins=[gbuf_in[:, :]], outs=[gbuf_out[:, :]]))

                    cc_done = CC.v
                    ld_g1 = LD.v

                    @block.sync
                    def _(sy):
                        sy.wait_ge(CC.h, cc_done)
                        for jh in range(2):
                            LD.inc(sy.dma_start(out=dW2a[:, jh * 1024:(jh + 1) * 1024],
                                                in_=gbuf_out[jh * 128:(jh + 1) * 128, :]), 16)
                        for dc in range(8):
                            LD.inc(sy.dma_start(
                                out=dW1a[:, dc * 256:(dc + 1) * 256],
                                in_=gbuf_out[256 + (dc // 4) * 128:256 + (dc // 4 + 1) * 128,
                                             (dc % 4) * 256:(dc % 4 + 1) * 256]), 16)
                        LD.inc(sy.dma_start(
                            out=dba[:, 0:8].rearrange("p (r c) -> p r c", r=1),
                            in_=gbuf_out[512:513, :].rearrange("r (p c) -> p r c", p=128)), 16)
                        LD.inc(sy.dma_start(
                            out=dba[:, 8:10].rearrange("p (r c) -> p r c", r=1),
                            in_=gbuf_out[513:514, 0:256].rearrange("r (p c) -> p r c", p=128)), 16)

                    @block.vector
                    def _(v):
                        v.wait_ge(LD.h, ld_g1 + 16 * 12)
                        CH.inc(v.engine_nop())

                    probe("ar%d" % pass_id, lambda: dW1a[:], [128, 8 * 256], F32)
                    probe("arb%d" % pass_id, lambda: dba[:], [128, 10], F32)

                def emit_update(ucol):
                    @block.vector
                    def _(v):
                        v.wait_ge(CH.h, CH.v)
                        v.scalar_tensor_tensor(out=W1T_f[:], in0=dW1a[:], scalar=ucol,
                                               in1=W1T_f[:], op0=ALU.mult, op1=ALU.add)
                        v.tensor_copy(W1T_b[:], W1T_f[:])
                        v.scalar_tensor_tensor(out=W2T_f[:], in0=dW2a[:], scalar=ucol,
                                               in1=W2T_f[:], op0=ALU.mult, op1=ALU.add)
                        v.tensor_copy(W2T_b[:], W2T_f[:])
                        v.scalar_tensor_tensor(out=b2col[:, 0:8], in0=dba[:, 0:8],
                                               scalar=ucol, in1=b2col[:, 0:8],
                                               op0=ALU.mult, op1=ALU.add)
                        CH.inc(v.scalar_tensor_tensor(out=b1col[:], in0=dba[:, 8:10],
                                                      scalar=ucol, in1=b1col[:],
                                                      op0=ALU.mult, op1=ALU.add))

                    @block.vector
                    def _(v):
                        v.wait_ge(CH.h, CH.v)
                        CH.inc(v.tensor_scalar(out=b2col[:, 8:16], in0=b2col[:, 0:8],
                                               scalar1=NDIV, scalar2=None, op0=ALU.mult))
                    for dc in range(8):
                        for jh in range(2):
                            @block.tensor
                            def _(t, dc=dc, jh=jh):
                                t.wait_ge(CH.h, CH.v)
                                CH.inc(t.transpose(PB[3][:, 0:128],
                                                   dW2a[:, jh * 1024 + dc * 128:jh * 1024 + (dc + 1) * 128],
                                                   identf[:]))

                            @block.vector
                            def _(v, dc=dc, jh=jh):
                                v.wait_ge(CH.h, CH.v)
                                o = dc * 256 + jh * 128
                                CH.inc(v.scalar_tensor_tensor(
                                    out=W2n_f[:, o:o + 128], in0=PB[3][:, 0:128],
                                    scalar=ucol, in1=W2n_f[:, o:o + 128],
                                    op0=ALU.mult, op1=ALU.add))

                    @block.vector
                    def _(v):
                        v.wait_ge(CH.h, CH.v)
                        CH.inc(v.tensor_copy(W2n_b[:], W2n_f[:]))

                emit_grad_pass._n = 0
                emit_grad_pass(0, W1T_b, W2T_b, W2n_b, b1col,
                               mskc[:, 0:1], g1i_d, g1o_d)
                emit_update(-LR)
                probe("W1Tf_1", lambda: W1T_f[:], [128, 8 * 256], F32)
                probe("W2nf_1", lambda: W2n_f[:], [128, 8 * 256], F32)
                emit_grad_pass(1, W1T_b, W2T_b, W2n_b, b1col,
                               mskc[:, 1:2], g2i_d, g2o_d)
                emit_update(mskc[:, 2:3])

                es_g.close()
                es_r = contextlib.ExitStack()
                tq_b = sb("tq_b", [128, 8 * 1024], BF16, es_r)
                q_f = sb("q_f", [128, 8 * 512], F32, es_r)
                q_b = sb("q_b", [128, 8 * 512], BF16, es_r)
                t_fm = sb("t_fm", [128, 8 * 512], F32, es_r)
                t_tm = sb("t_tm", [128, 4 * 1024], F32, es_r)

                lw_r0 = LW.v

                @block.gpsimd
                def _(g):
                    g.wait_ge(CH.h, CH.v)
                    for kc in range(8):
                        LW.inc(g.dma_start(out=tq_b[:, kc * 1024:(kc + 1) * 1024],
                                           in_=tqT[kc * 128:(kc + 1) * 128, :]), 16)

                @block.tensor
                def _(t):
                    t.wait_ge(LW.h, LW.v)

                for stt_i in range(4):
                    tok0 = stt_i * 512
                    for dc in range(8):
                        @block.tensor
                        def _(t, dc=dc):
                            t.wait_ge(CH.h, CH.v)
                            for kc in range(8):
                                ins = t.matmul(PB[dc % 2][:],
                                               tq_b[:, kc * 1024 + dc * 128:kc * 1024 + (dc + 1) * 128],
                                               xn_own[:, kc * RPC + tok0:kc * RPC + tok0 + 512],
                                               start=(kc == 0), stop=(kc == 7))
                            CH.inc(ins)

                        @block.scalar
                        def _(s, dc=dc):
                            s.wait_ge(CH.h, CH.v)
                            CH.inc(s.activation(out=q_f[:, dc * 512:(dc + 1) * 512],
                                                in_=PB[dc % 2][:], func=AF.Copy))

                        @block.vector
                        def _(v, dc=dc):
                            v.wait_ge(CH.h, CH.v)
                            CH.inc(v.tensor_copy(q_b[:, dc * 512:(dc + 1) * 512],
                                                 q_f[:, dc * 512:(dc + 1) * 512]))

                    for jh in range(2):
                        @block.tensor
                        def _(t, jh=jh):
                            t.wait_ge(CH.h, CH.v)
                            for kc in range(8):
                                ins = t.matmul(PB[2][:],
                                               W1T_b[:, kc * 256 + jh * 128:kc * 256 + (jh + 1) * 128],
                                               q_b[:, kc * 512:(kc + 1) * 512],
                                               start=(kc == 0), stop=(kc == 7))
                            CH.inc(ins)

                        @block.scalar
                        def _(s, jh=jh):
                            s.wait_ge(CH.h, CH.v)
                            CH.inc(s.activation(out=hg_bf[:, jh * 512:(jh + 1) * 512],
                                                in_=PB[2][:], func=AF.Gelu,
                                                bias=b1col[:, jh:jh + 1]))

                    for dc in range(8):
                        @block.tensor
                        def _(t, dc=dc):
                            t.wait_ge(CH.h, CH.v)
                            for jh in range(2):
                                ins = t.matmul(PB[dc % 2][:],
                                               W2T_b[:, jh * 1024 + dc * 128:jh * 1024 + (dc + 1) * 128],
                                               hg_bf[:, jh * 512:(jh + 1) * 512],
                                               start=(jh == 0), stop=(jh == 1))
                            CH.inc(ins)

                        @block.vector
                        def _(v, dc=dc):
                            v.wait_ge(CH.h, CH.v)
                            CH.inc(v.scalar_tensor_tensor(
                                out=t_fm[:, dc * 512:(dc + 1) * 512],
                                in0=PB[dc % 2][:], scalar=b2col[:, dc:dc + 1],
                                in1=q_f[:, dc * 512:(dc + 1) * 512],
                                op0=ALU.add, op1=ALU.add))

                    for nt in range(4):
                        @block.tensor
                        def _(t, nt=nt):
                            t.wait_ge(CH.h, CH.v)
                            for dc in range(8):
                                ins = t.transpose(
                                    PB[2 + dc // 4][:, (dc % 4) * 128:(dc % 4 + 1) * 128],
                                    t_fm[:, dc * 512 + nt * 128:dc * 512 + (nt + 1) * 128],
                                    identf[:])
                            CH.inc(ins)

                        @block.vector
                        def _(v, nt=nt):
                            v.wait_ge(CH.h, CH.v)
                            for dc in range(8):
                                ins = v.tensor_copy(
                                    t_tm[:, nt * 1024 + dc * 128:nt * 1024 + (dc + 1) * 128],
                                    PB[2 + dc // 4][:, (dc % 4) * 128:(dc % 4 + 1) * 128])
                            CH.inc(ins)

                        @block.sync
                        def _(sy, stt_i=stt_i, nt=nt):
                            sy.wait_ge(CH.h, CH.v)
                            LD.inc(sy.dma_start(
                                out=ttt_d[stt_i * 512 + nt * 128:stt_i * 512 + (nt + 1) * 128, :],
                                in_=t_tm[:, nt * 1024:(nt + 1) * 1024]), 16)

            if MAXPH >= 2:
                es_r.close()
            else:
                es_a0.close()
            es_ttt.close()
            ch_ttt_end = CH.v
            ld_ttt_end = LD.v

            if MAXPH < 6:
                @block.sync
                def _(sy):
                    sy.wait_ge(CH.h, ch_ttt_end)
                    sy.wait_ge(LD.h, ld_ttt_end)
                    for i in range(16):
                        LD.inc(sy.dma_start(out=out_o[i * 128:(i + 1) * 128, :],
                                            in_=ttt_d[i * 128:(i + 1) * 128, :]), 16)
                    sy.wait_ge(LD.h, LD.v)

            if MAXPH >= 3:
                es_att = contextlib.ExitStack()
                qs_all = sb("qs_all", [128, 2 * S], BF16, es_att)
                ks_all = sb("ks_all", [128, 2 * S], BF16, es_att)
                vp = sb("vp", [128, 64 * 260], BF16, es_att)
                Wqkv_b = sb("Wqkv_b", [128, 8 * 768], BF16, es_att)
                xc = sb("xc", [128, 2 * 4096], F32, es_att)
                xsq4 = sb("xsq4", [128, 4096], F32, es_att)
                u4 = sb("u4", [128, 4096], BF16, es_att)
                hbuf = sb("hbuf", [128, 8 * 512], BF16, es_att)
                sqb = sb("sqb", [128, 512], BF16, es_att)
                rbb = sb("rbb", [128, 512], BF16, es_att)
                vfm = sb("vfm", [128, 512], BF16, es_att)
                rws2 = sb("rws2", [128, 16], F32, es_att)
                rws3 = sb("rws3", [128, 16], F32, es_att)
                nrows = sb("nrows", [1, 4 * 512], F32, es_att)
                rrowb = sb("rrowb", [1, 2 * 512], BF16, es_att)
                mrow = sb("mrow", [1, 64], F32, es_att)
                Mb = sb("Mb", [128, 4], F32, es_att)
                ptb = sb("ptbuf", [128, 2 * 512], BF16, es_att)
                atst = sb("atst", [128, 2 * 512], BF16, es_att)
                rrowb2 = sb("rrowb2", [1, 512], BF16, es_att)
                rbb2 = sb("rbb2", [128, 512], BF16, es_att)

                lw_p0 = LW.v

                @block.gpsimd
                def _(g):
                    g.wait_ge(CH.h, CH.v)
                    for kc in range(8):
                        LW.inc(g.dma_start(out=Wqkv_b[:, kc * 768:kc * 768 + 256],
                                           in_=Wq_i[kc * 128:(kc + 1) * 128, :]), 16)
                        LW.inc(g.dma_start(out=Wqkv_b[:, kc * 768 + 256:kc * 768 + 512],
                                           in_=Wk_i[kc * 128:(kc + 1) * 128, :]), 16)
                        LW.inc(g.dma_start(out=Wqkv_b[:, kc * 768 + 512:kc * 768 + 768],
                                           in_=Wv_i[kc * 128:(kc + 1) * 128, :]), 16)

                lw_prep = LW.v

                @block.vector
                def _(v):
                    v.wait_ge(CH.h, CH.v)
                    v.wait_ge(LW.h, lw_prep)
                    vp3 = vp[:].rearrange("p (t c) -> p t c", c=260)
                    for lh in range(4):
                        ins = v.memset(vp3[:, :, 65 * lh + 64:65 * lh + 65], 1.0)
                    CH.inc(ins)

                ld_prep0 = LD.v
                prep_done = {}
                for tc in range(16):
                    @block.sync
                    def _(sy, tc=tc):
                        if tc >= 2:
                            sy.wait_ge(CH.h, prep_done[tc - 2])
                        LD.inc(sy.dma_start(
                            out=xc[:, (tc % 2) * 4096:(tc % 2 + 1) * 4096].rearrange(
                                "p (i c) -> p i c", i=4),
                            in_=x_full[tc * 512:(tc + 1) * 512, :].rearrange(
                                "(i p) c -> p i c", p=128)), 16)

                    xcc = xc[:, (tc % 2) * 4096:(tc % 2 + 1) * 4096]

                    @block.vector
                    def _(v, tc=tc, xcc=xcc):
                        v.wait_ge(LD.h, ld_prep0 + 16 * (tc + 1))
                        v.wait_ge(CH.h, CH.v)
                        for i in range(4):
                            ins = v.reduce_sum(rws[:, i:i + 1],
                                               xcc[:, i * 1024:(i + 1) * 1024], axis=AX)
                        CH.inc(ins)

                    @block.scalar
                    def _(s, xcc=xcc):
                        s.wait_ge(CH.h, CH.v)
                        CH.inc(s.activation(out=xsq4[:], in_=xcc, func=AF.Square))

                    @block.vector
                    def _(v):
                        v.wait_ge(CH.h, CH.v)
                        for i in range(4):
                            v.reduce_sum(rws[:, 4 + i:5 + i],
                                         xsq4[:, i * 1024:(i + 1) * 1024], axis=AX)
                        CH.inc(v.tensor_scalar(out=rws[:, 8:12], in0=rws[:, 0:4],
                                               scalar1=1.0 / D, scalar2=None,
                                               op0=ALU.mult))

                    @block.scalar
                    def _(s):
                        s.wait_ge(CH.h, CH.v)
                        CH.inc(s.activation(out=rws[:, 12:16], in_=rws[:, 8:12],
                                            func=AF.Square))

                    @block.vector
                    def _(v):
                        v.wait_ge(CH.h, CH.v)
                        CH.inc(v.scalar_tensor_tensor(out=rws2[:, 0:4], in0=rws[:, 4:8],
                                                      scalar=1.0 / D, in1=rws[:, 12:16],
                                                      op0=ALU.mult, op1=ALU.subtract))

                    @block.scalar
                    def _(s):
                        s.wait_ge(CH.h, CH.v)
                        CH.inc(s.activation(out=rws2[:, 4:8], in_=rws2[:, 0:4],
                                            func=AF.Sqrt, bias=EPS))

                    @block.vector
                    def _(v):
                        v.wait_ge(CH.h, CH.v)
                        CH.inc(v.reciprocal(rws2[:, 8:12], rws2[:, 4:8]))

                    @block.scalar
                    def _(s):
                        s.wait_ge(CH.h, CH.v)
                        CH.inc(s.activation(out=rws2[:, 12:16], in_=rws2[:, 8:12],
                                            func=AF.Square))

                    @block.vector
                    def _(v):
                        v.wait_ge(CH.h, CH.v)
                        CH.inc(v.scalar_tensor_tensor(out=rws3[:, 0:4], in0=rws2[:, 0:4],
                                                      scalar=EPS, in1=rws2[:, 12:16],
                                                      op0=ALU.subtract, op1=ALU.mult))

                    @block.scalar
                    def _(s):
                        s.wait_ge(CH.h, CH.v)
                        CH.inc(s.activation(out=rws3[:, 4:8], in_=rws3[:, 0:4],
                                            func=AF.Sqrt, bias=EPS))

                    @block.vector
                    def _(v):
                        v.wait_ge(CH.h, CH.v)
                        CH.inc(v.reciprocal(rws3[:, 8:12], rws3[:, 4:8]))

                    @block.vector
                    def _(v):
                        v.wait_ge(CH.h, CH.v)
                        CH.inc(v.tensor_mul(rws3[:, 12:16], rws2[:, 8:12],
                                            rws3[:, 8:12]))

                    @block.vector
                    def _(v, xcc=xcc):
                        v.wait_ge(CH.h, CH.v)
                        for i in range(4):
                            ins = v.tensor_scalar(
                                out=u4[:, i * 1024:(i + 1) * 1024],
                                in0=xcc[:, i * 1024:(i + 1) * 1024],
                                scalar1=rws[:, 8 + i:9 + i], scalar2=rws3[:, 12 + i:13 + i],
                                op0=ALU.subtract, op1=ALU.mult)
                        CH.inc(ins)

                    for i in range(4):
                        @block.tensor
                        def _(t, i=i):
                            t.wait_ge(CH.h, CH.v)
                            for kc in range(8):
                                ins = t.transpose(PT[:, kc * 128:(kc + 1) * 128],
                                                  u4[:, i * 1024 + kc * 128:i * 1024 + (kc + 1) * 128],
                                                  identb[:])
                            CH.inc(ins)

                        @block.vector
                        def _(v, i=i):
                            v.wait_ge(CH.h, CH.v)
                            for kc in range(8):
                                ins = v.tensor_scalar(
                                    out=hbuf[:, kc * 512 + i * 128:kc * 512 + (i + 1) * 128],
                                    in0=PT[:, kc * 128:(kc + 1) * 128],
                                    scalar1=lngc[:, kc:kc + 1], scalar2=lnbc[:, kc:kc + 1],
                                    op0=ALU.mult, op1=ALU.add)
                            CH.inc(ins)

                    for m in range(6):
                        @block.tensor
                        def _(t, m=m):
                            t.wait_ge(CH.h, CH.v)
                            for kc in range(8):
                                ins = t.matmul(PB[m][:],
                                               Wqkv_b[:, kc * 768 + m * 128:kc * 768 + (m + 1) * 128],
                                               hbuf[:, kc * 512:(kc + 1) * 512],
                                               start=(kc == 0), stop=(kc == 7))
                            CH.inc(ins)

                    for m in range(4):
                        ch_ = m % 2
                        dst = qs_all if m < 2 else ks_all
                        scol = qscol if m < 2 else kscol

                        @block.scalar
                        def _(s, m=m):
                            s.wait_ge(CH.h, CH.v)
                            CH.inc(s.activation(out=sqb[:], in_=PB[m][:],
                                                func=AF.Square))

                        @block.tensor
                        def _(t):
                            t.wait_ge(CH.h, CH.v)
                            CH.inc(t.matmul(PB[6][0:65, :], on2w_b[:], sqb[:],
                                            start=True, stop=True))

                        @block.vector
                        def _(v):
                            v.wait_ge(CH.h, CH.v)
                            v.tensor_scalar(out=nrows[:, 0:512], in0=PB[6][0:1, :],
                                            scalar1=L2EPS, scalar2=None, op0=ALU.add)
                            CH.inc(v.tensor_scalar(out=nrows[:, 512:1024],
                                                   in0=PB[6][64:65, :],
                                                   scalar1=L2EPS, scalar2=None,
                                                   op0=ALU.add))

                        @block.scalar
                        def _(s):
                            s.wait_ge(CH.h, CH.v)
                            s.activation(out=nrows[:, 1024:1536], in_=nrows[:, 0:512],
                                         func=AF.Sqrt)
                            CH.inc(s.activation(out=nrows[:, 1536:2048],
                                                in_=nrows[:, 512:1024], func=AF.Sqrt))

                        @block.vector
                        def _(v):
                            v.wait_ge(CH.h, CH.v)
                            with nc.allow_low_precision(reason="bf16 matmul operand"):
                                v.reciprocal(rrowb[:, 0:512], nrows[:, 1024:1536])
                                CH.inc(v.reciprocal(rrowb[:, 512:1024], nrows[:, 1536:2048]))

                        @block.tensor
                        def _(t):
                            t.wait_ge(CH.h, CH.v)
                            CH.inc(t.matmul(PB[6][0:64, :], on1_64b[:],
                                            rrowb[:, 0:512], start=True, stop=True))

                        @block.vector
                        def _(v):
                            v.wait_ge(CH.h, CH.v)
                            CH.inc(v.tensor_copy(rbb[0:64, :], PB[6][0:64, :]))

                        @block.tensor
                        def _(t):
                            t.wait_ge(CH.h, CH.v)
                            CH.inc(t.matmul(PB[6][0:64, :], on1_64b[:],
                                            rrowb[:, 512:1024], start=True, stop=True))

                        @block.vector
                        def _(v):
                            v.wait_ge(CH.h, CH.v)
                            CH.inc(v.tensor_copy(rbb[64:128, :], PB[6][0:64, :]))

                        @block.vector
                        def _(v, m=m, tc=tc, dst=dst, scol=scol, ch_=ch_):
                            v.wait_ge(CH.h, CH.v)
                            CH.inc(v.scalar_tensor_tensor(
                                out=dst[:, ch_ * S + tc * 512:ch_ * S + (tc + 1) * 512],
                                in0=PB[m][:], scalar=scol[:, ch_:ch_ + 1], in1=rbb[:],
                                op0=ALU.mult, op1=ALU.mult))

                    for m in (4, 5):
                        @block.vector
                        def _(v, m=m):
                            v.wait_ge(CH.h, CH.v)
                            CH.inc(v.tensor_copy(vfm[:], PB[m][:]))

                        for j in range(4):
                            @block.tensor
                            def _(t, j=j):
                                t.wait_ge(CH.h, CH.v)
                                CH.inc(t.transpose(PT[:, 0:128],
                                                   vfm[:, j * 128:(j + 1) * 128],
                                                   identb[:]))

                            @block.vector
                            def _(v, m=m, tc=tc, j=j):
                                v.wait_ge(CH.h, CH.v)
                                tg = tc * 4 + j
                                lh0 = 2 * (m - 4)
                                v.tensor_copy(vp[:, tg * 260 + 65 * lh0:tg * 260 + 65 * lh0 + 64],
                                              PT[:, 0:64])
                                CH.inc(v.tensor_copy(
                                    vp[:, tg * 260 + 65 * (lh0 + 1):tg * 260 + 65 * (lh0 + 1) + 64],
                                    PT[:, 64:128]))

                    prep_done[tc] = CH.v

            if MAXPH >= 3:
                probe("qsa", lambda: qs_all[:], [128, 2 * S], BF16)
                probe("ksa", lambda: ks_all[:], [128, 2 * S], BF16)
                probe("vpa", lambda: vp[:], [128, 64 * 260], BF16)

            if MAXPH >= 4:
                sqall = xsq4[:].bitcast(BF16)
                for si, src in enumerate((qs_all, ks_all)):
                    for ch_ in range(2):
                        @block.scalar
                        def _(s, src=src, ch_=ch_):
                            s.wait_ge(CH.h, CH.v)
                            CH.inc(s.activation(out=sqall, in_=src[:, ch_ * S:(ch_ + 1) * S],
                                                func=AF.Square))

                        for tc in range(16):
                            @block.tensor
                            def _(t, tc=tc):
                                t.wait_ge(CH.h, CH.v)
                                CH.inc(t.matmul(PB[6][0:65, :], on2w_b[:],
                                                sqall[:, tc * 512:(tc + 1) * 512],
                                                start=True, stop=True))

                            @block.vector
                            def _(v, tc=tc):
                                v.wait_ge(CH.h, CH.v)
                                v.reduce_max(mrow[:, 16 + tc:17 + tc],
                                             PB[6][0:1, :], axis=AX)
                                CH.inc(v.reduce_max(mrow[:, 32 + tc:33 + tc],
                                                    PB[6][64:65, :], axis=AX))

                        @block.vector
                        def _(v, si=si, ch_=ch_):
                            v.wait_ge(CH.h, CH.v)
                            base = si * 4
                            v.reduce_max(mrow[:, base + 2 * ch_:base + 2 * ch_ + 1],
                                         mrow[:, 16:32], axis=AX)
                            CH.inc(v.reduce_max(mrow[:, base + 2 * ch_ + 1:base + 2 * ch_ + 2],
                                                mrow[:, 32:48], axis=AX))

                @block.vector
                def _(v):
                    v.wait_ge(CH.h, CH.v)
                    CH.inc(v.tensor_mul(mrow[:, 8:12], mrow[:, 0:4], mrow[:, 4:8]))

                @block.scalar
                def _(s):
                    s.wait_ge(CH.h, CH.v)
                    CH.inc(s.activation(out=mrow[:, 12:16], in_=mrow[:, 8:12],
                                        func=AF.Sqrt))

                @block.vector
                def _(v):
                    v.wait_ge(CH.h, CH.v)
                    CH.inc(v.tensor_scalar(out=mrow[:, 16:20], in0=mrow[:, 12:16],
                                           scalar1=-10.0, scalar2=None, op0=ALU.mult))

                @block.tensor
                def _(t):
                    t.wait_ge(CH.h, CH.v)
                    CH.inc(t.matmul(PB[6][:, 0:4], on1_128f[:], mrow[:, 16:20],
                                    start=True, stop=True))

                @block.vector
                def _(v):
                    v.wait_ge(CH.h, CH.v)
                    CH.inc(v.tensor_copy(Mb[:, 0:4], PB[6][:, 0:4]))

                probe("mb", lambda: Mb[:], [128, 4], F32)
                ch_att0 = CH.v

                n_base = 0
                ld_att0 = LD.v
                g_idx = 0
                for qw in range(16):
                    kts = 4 * qw + 4
                    for lh in range(4):
                        ch_ = lh // 2
                        par = lh % 2
                        pa, pb_ = 64 * par, 64 * par + 64
                        ab = 4 + (g_idx % 2)

                        @block.tensor
                        def _(t, qw=qw, lh=lh, kts=kts, n_base=n_base,
                              g_idx=g_idx, ch_=ch_, pa=pa, pb_=pb_, ab=ab):
                            t.wait_ge(CH.h, ch_att0)
                            if g_idx >= 2:
                                t.wait_ge(SEV.h, g_idx - 1)
                            qsl = qs_all[pa:pb_, ch_ * S + qw * 512:ch_ * S + (qw + 1) * 512]

                            def scores(kk):
                                nn = n_base + kk
                                if nn >= 2:
                                    t.wait_ge(SX.h, nn - 1)
                                ins = t.matmul(PB[nn % 2][:],
                                               ks_all[pa:pb_, ch_ * S + kk * 128:ch_ * S + (kk + 1) * 128],
                                               qsl, start=True, stop=True,
                                               skip_group_check=True)
                                SM.inc(ins)

                            scores(0)
                            for kt in range(kts):
                                n = n_base + kt
                                if kt + 1 < kts:
                                    scores(kt + 1)
                                t.wait_ge(SX.h, n + 1)
                                ins = t.matmul(PB[ab][0:65, :],
                                               vp[:, kt * 260 + 65 * lh:kt * 260 + 65 * lh + 65],
                                               ptb[:, (n % 2) * 512:(n % 2 + 1) * 512],
                                               start=(kt == 0), stop=(kt == kts - 1),
                                               skip_group_check=True)
                                SPV.inc(ins)

                        @block.scalar
                        def _(s, qw=qw, lh=lh, kts=kts, n_base=n_base):
                            for kt in range(kts):
                                n = n_base + kt
                                s.wait_ge(SM.h, n + 1)
                                if n >= 2:
                                    s.wait_ge(SPV.h, n - 1)
                                ins = s.activation(out=ptb[:, (n % 2) * 512:(n % 2 + 1) * 512],
                                                   in_=PB[n % 2][:], func=AF.Exp,
                                                   scale=10.0, bias=Mb[:, lh:lh + 1])
                                if kt >= 4 * qw:
                                    SXA.inc(ins)
                                else:
                                    SX.inc(ins)

                        sxa_base = SXA.v - 4

                        @block.vector
                        def _(v, qw=qw, kts=kts, n_base=n_base, sxa_base=sxa_base):
                            for i in range(4):
                                kt = 4 * qw + i
                                n = n_base + kt
                                v.wait_ge(SXA.h, sxa_base + i + 1)
                                ins = v.tensor_mul(ptb[:, (n % 2) * 512:(n % 2 + 1) * 512],
                                                   ptb[:, (n % 2) * 512:(n % 2 + 1) * 512],
                                                   cmask[:, i * 512:(i + 1) * 512])
                                SX.inc(ins)

                        @block.vector
                        def _(v, g_idx=g_idx, ab=ab):
                            v.wait_ge(SPV.h, SPV.v)
                            if g_idx >= 2:
                                v.wait_ge(LD.h, ld_att0 + 16 * (g_idx - 1))
                            with nc.allow_low_precision(reason="bf16 matmul operand"):
                                CH.inc(v.reciprocal(rrowb2[:], PB[ab][64:65, :]))

                        @block.tensor
                        def _(t):
                            t.wait_ge(CH.h, CH.v)
                            CH.inc(t.matmul(PB[6][0:64, :], on1_64b[:], rrowb2[:],
                                            start=True, stop=True))

                        @block.vector
                        def _(v):
                            v.wait_ge(CH.h, CH.v)
                            CH.inc(v.tensor_copy(rbb2[0:64, :], PB[6][0:64, :]))

                        @block.vector
                        def _(v, g_idx=g_idx, ab=ab):
                            v.wait_ge(CH.h, CH.v)
                            ins = v.scalar_tensor_tensor(
                                out=atst[0:64, (g_idx % 2) * 512:(g_idx % 2 + 1) * 512],
                                in0=PB[ab][0:64, :], scalar=1.0, in1=rbb2[0:64, :],
                                op0=ALU.mult, op1=ALU.mult)
                            SEV.inc(ins)

                        @block.sync
                        def _(sy, qw=qw, lh=lh, g_idx=g_idx):
                            sy.wait_ge(SEV.h, g_idx + 1)
                            LD.inc(sy.dma_start(
                                out=agi_d[64 * lh:64 * (lh + 1), qw * 512:(qw + 1) * 512],
                                in_=atst[0:64, (g_idx % 2) * 512:(g_idx % 2 + 1) * 512]), 16)

                        n_base += kts
                        g_idx += 1

                ld_ag = LD.v
                if "agi_d" in dbg_outs:
                    @block.sync
                    def _(sy):
                        sy.wait_ge(LD.h, ld_ag)
                        for rb in range(2):
                            LD.inc(sy.dma_start(
                                out=dbg_outs["agi_d"][rb * 128:(rb + 1) * 128, :],
                                in_=agi_d[rb * 128:(rb + 1) * 128, :]), 16)
                    ld_ag = LD.v

                @block.gpsimd
                def _(g):
                    g.wait_ge(LD.h, ld_ag)
                    CC.inc(g.collective_compute(
                        "AllGather", ALU.bypass,
                        replica_groups=[[0, 1, 2, 3], [4, 5, 6, 7]],
                        ins=[agi_d[:, :]], outs=[ago_d[:, :]]))

            if MAXPH >= 5:
                es_att.close()
                es_tail = contextlib.ExitStack()
                eg = sb("eg", [8, RPC], F32, es_tail)
                eg_b = sb("eg_b", [8, RPC], BF16, es_tail)
                rc05 = sb("rc05", [1, RPC], F32, es_tail)
                lr_ = sb("lr_", [1, 8 * 512], F32, es_tail)
                es_wo = contextlib.ExitStack()
                attn_all = sb("attn_all", [128, 8 * RPC], BF16, es_wo)
                y_sb = sb("y_sb", [128, 8 * RPC], F32, es_wo)
                Wo_b = sb("Wo_b", [128, 8 * 1024], BF16, es_wo)
                h2tt = sb("h2tt", [128, 8 * 512], BF16, es_wo)
                tb1 = sb("tb1", [128, 512], F32, es_wo)
                tb2 = sb("tb2", [128, 512], F32, es_wo)
                sqf = sb("sqf", [128, 512], F32, es_wo)

                cc_ag = CC.v

                @block.gpsimd
                def _(g):
                    g.wait_ge(CH.h, CH.v)
                    for kc in range(8):
                        LW.inc(g.dma_start(out=Wo_b[:, kc * 1024:(kc + 1) * 1024],
                                           in_=Wo_i[kc * 128:(kc + 1) * 128, :]), 16)

                ld_wo0 = LD.v

                @block.sync
                def _(sy):
                    sy.wait_ge(CH.h, CH.v)
                    sy.wait_ge(CC.h, cc_ag)
                    voff = sy.value_load(rofft[0:1, 0:1])
                    for kc in range(8):
                        LD.inc(sy.dma_start(
                            out=attn_all[:, kc * RPC:(kc + 1) * RPC],
                            in_=ago_d[kc * 128:(kc + 1) * 128, bass.ds(voff, RPC)]), 16)

                lw_wo = LW.v

                @block.vector
                def _(v):
                    v.wait_ge(LD.h, ld_wo0 + 16 * 8)
                    v.wait_ge(LW.h, lw_wo)
                    CH.inc(v.engine_nop())

                for tt in range(4):
                    for dc in range(8):
                        @block.tensor
                        def _(t, tt=tt, dc=dc):
                            t.wait_ge(CH.h, CH.v)
                            for kc in range(8):
                                ins = t.matmul(PB[dc % 2][:],
                                               Wo_b[:, kc * 1024 + dc * 128:kc * 1024 + (dc + 1) * 128],
                                               attn_all[:, kc * RPC + tt * 512:kc * RPC + (tt + 1) * 512],
                                               start=(kc == 0), stop=(kc == 7))
                            CH.inc(ins)

                        @block.scalar
                        def _(s, tt=tt, dc=dc):
                            s.wait_ge(CH.h, CH.v)
                            CH.inc(s.activation(out=y_sb[:, dc * RPC + tt * 512:dc * RPC + (tt + 1) * 512],
                                                in_=PB[dc % 2][:], func=AF.Copy))

                    for kc in range(8):
                        @block.scalar
                        def _(s, tt=tt, kc=kc):
                            s.wait_ge(CH.h, CH.v)
                            CH.inc(s.activation(out=sqf[:],
                                                in_=y_sb[:, kc * RPC + tt * 512:kc * RPC + (tt + 1) * 512],
                                                func=AF.Square))

                        @block.tensor
                        def _(t, tt=tt, kc=kc):
                            t.wait_ge(CH.h, CH.v)
                            t.matmul(PB[2][0:1, :], onc_f[:],
                                     y_sb[:, kc * RPC + tt * 512:kc * RPC + (tt + 1) * 512],
                                     start=(kc == 0), stop=(kc == 7),
                                     skip_group_check=True)
                            CH.inc(t.matmul(PB[3][0:1, :], onc_f[:], sqf[:],
                                            start=(kc == 0), stop=(kc == 7),
                                            skip_group_check=True))

                    @block.vector
                    def _(v):
                        v.wait_ge(CH.h, CH.v)
                        CH.inc(v.tensor_scalar(out=lr_[:, 0:512], in0=PB[2][0:1, :],
                                               scalar1=1.0 / D, scalar2=None,
                                               op0=ALU.mult))

                    @block.scalar
                    def _(s):
                        s.wait_ge(CH.h, CH.v)
                        CH.inc(s.activation(out=lr_[:, 512:1024], in_=lr_[:, 0:512],
                                            func=AF.Square))

                    @block.vector
                    def _(v):
                        v.wait_ge(CH.h, CH.v)
                        CH.inc(v.scalar_tensor_tensor(out=lr_[:, 1024:1536],
                                                      in0=PB[3][0:1, :], scalar=1.0 / D,
                                                      in1=lr_[:, 512:1024],
                                                      op0=ALU.mult, op1=ALU.subtract))

                    @block.scalar
                    def _(s):
                        s.wait_ge(CH.h, CH.v)
                        CH.inc(s.activation(out=lr_[:, 1536:2048], in_=lr_[:, 1024:1536],
                                            func=AF.Sqrt, bias=EPS))

                    @block.vector
                    def _(v):
                        v.wait_ge(CH.h, CH.v)
                        CH.inc(v.reciprocal(lr_[:, 2048:2560], lr_[:, 1536:2048]))

                    @block.vector
                    def _(v):
                        v.wait_ge(CH.h, CH.v)
                        CH.inc(v.scalar_tensor_tensor(out=lr_[:, 2560:3072],
                                                      in0=lr_[:, 0:512], scalar=-1.0,
                                                      in1=lr_[:, 2048:2560],
                                                      op0=ALU.mult, op1=ALU.mult))

                    @block.tensor
                    def _(t):
                        t.wait_ge(CH.h, CH.v)
                        t.matmul(PB[4][:], on1_128f[:], lr_[:, 2048:2560],
                                 start=True, stop=True)
                        CH.inc(t.matmul(PB[5][:], on1_128f[:], lr_[:, 2560:3072],
                                        start=True, stop=True))

                    for kc in range(8):
                        @block.vector
                        def _(v, tt=tt, kc=kc):
                            v.wait_ge(CH.h, CH.v)
                            v.tensor_mul(tb1[:], y_sb[:, kc * RPC + tt * 512:kc * RPC + (tt + 1) * 512],
                                         PB[4][:])
                            v.tensor_add(tb2[:], tb1[:], PB[5][:])
                            CH.inc(v.tensor_scalar(out=h2tt[:, kc * 512:(kc + 1) * 512],
                                                   in0=tb2[:], scalar1=lngc[:, kc:kc + 1],
                                                   scalar2=lnbc[:, kc:kc + 1],
                                                   op0=ALU.mult, op1=ALU.add))

                        @block.sync
                        def _(sy, tt=tt, kc=kc):
                            sy.wait_ge(CH.h, CH.v)
                            LD.inc(sy.dma_start(
                                out=h2_d[kc * 128:(kc + 1) * 128, tt * 512:(tt + 1) * 512],
                                in_=h2tt[:, kc * 512:(kc + 1) * 512]), 16)

                    @block.tensor
                    def _(t):
                        t.wait_ge(CH.h, CH.v)
                        for kc in range(8):
                            ins = t.matmul(PB[6][0:8, :], gw_b[:, kc * 8:(kc + 1) * 8],
                                           h2tt[:, kc * 512:(kc + 1) * 512],
                                           start=(kc == 0), stop=(kc == 7))
                        CH.inc(ins)

                    @block.scalar
                    def _(s, tt=tt):
                        s.wait_ge(CH.h, CH.v)
                        CH.inc(s.activation(out=eg[:, tt * 512:(tt + 1) * 512],
                                            in_=PB[6][0:8, :], func=AF.Exp,
                                            bias=gbcol[:, 0:1]))

                    @block.vector
                    def _(v, tt=tt):
                        v.wait_ge(CH.h, CH.v)
                        CH.inc(v.tensor_copy(eg_b[:, tt * 512:(tt + 1) * 512],
                                             eg[:, tt * 512:(tt + 1) * 512]))

                    @block.tensor
                    def _(t, tt=tt):
                        t.wait_ge(CH.h, CH.v)
                        CH.inc(t.matmul(PB[2][0:1, :], on8f[:],
                                        eg[:, tt * 512:(tt + 1) * 512],
                                        start=True, stop=True))

                    @block.vector
                    def _(v):
                        v.wait_ge(CH.h, CH.v)
                        CH.inc(v.tensor_scalar(out=lr_[:, 0:512], in0=PB[2][0:1, :],
                                               scalar1=2.0, scalar2=None, op0=ALU.mult))

                    @block.vector
                    def _(v, tt=tt):
                        v.wait_ge(CH.h, CH.v)
                        CH.inc(v.reciprocal(rc05[:, tt * 512:(tt + 1) * 512],
                                            lr_[:, 0:512]))

            if MAXPH >= 5 and "h2_d" in dbg_outs:
                @block.sync
                def _(sy):
                    sy.wait_ge(CH.h, CH.v)
                    for rb in range(8):
                        LD.inc(sy.dma_start(
                            out=dbg_outs["h2_d"][rb * 128:(rb + 1) * 128, :],
                            in_=h2_d[rb * 128:(rb + 1) * 128, :]), 16)

            if MAXPH >= 6:
                es_wo.close()
                es_moe = contextlib.ExitStack()
                slot = [sb("slot%d" % i, [128, 16384], BF16, es_moe) for i in range(2)]
                h2m = sb("h2m", [128, 8 * 1024], BF16, es_moe)
                hid = sb("hid", [128, 16 * 1024], BF16, es_moe)
                macc = sb("macc", [128, 8 * 1024], F32, es_moe)
                gbsb = sb("gbsb", [128, 1024], F32, es_moe)
                rb05 = sb("rb05", [128, 1024], F32, es_moe)
                tev = sb("tev", [128, 2 * 512], F32, es_moe)
                hidf = hid[:].bitcast(F32)
                xo_v = lambda: hidf[:, 0:1024]
                tj_v = lambda: hidf[:, 1024:2048]
                ot_v = lambda: hidf[:, 2048:3072]
                ffn_v = lambda: hidf[:, 4096:8192]

                ch_moe0 = CH.v
                htile = [0]
                etile = [0]
                sl_done = []

                for tt in range(2):
                    @block.sync
                    def _(sy, tt=tt):
                        sy.wait_ge(CH.h, CH.v)
                        if tt == 1:
                            sy.wait_ge(MA.h, MA.v)
                        for kc in range(8):
                            LD.inc(sy.dma_start(
                                out=h2m[:, kc * 1024:(kc + 1) * 1024],
                                in_=h2_d[kc * 128:(kc + 1) * 128, tt * 1024:(tt + 1) * 1024]), 16)

                    ld_h2m = LD.v

                    for th in range(2):
                        @block.tensor
                        def _(t, tt=tt, th=th):
                            t.wait_ge(CH.h, CH.v)
                            if tt == 1 and th == 0:
                                t.wait_ge(MA.h, MA.v)
                            CH.inc(t.matmul(PB[5][:], on1_128f[:],
                                            rc05[:, tt * 1024 + th * 512:tt * 1024 + (th + 1) * 512],
                                            start=True, stop=True))

                        @block.vector
                        def _(v, th=th):
                            v.wait_ge(CH.h, CH.v)
                            CH.inc(v.tensor_copy(rb05[:, th * 512:(th + 1) * 512],
                                                 PB[5][:]))

                    for dc in range(8):
                        for th in range(2):
                            @block.tensor
                            def _(t, tt=tt, dc=dc, th=th):
                                t.wait_ge(CH.h, CH.v)
                                CH.inc(t.matmul(PB[6][:], b2m_b[:, dc * 128:(dc + 1) * 128],
                                                eg_b[:, tt * 1024 + th * 512:tt * 1024 + (th + 1) * 512],
                                                start=True, stop=True))

                            @block.vector
                            def _(v, dc=dc, th=th):
                                v.wait_ge(CH.h, CH.v)
                                CH.inc(v.tensor_mul(macc[:, dc * 1024 + th * 512:dc * 1024 + (th + 1) * 512],
                                                    PB[6][:], rb05[:, th * 512:(th + 1) * 512]))

                    ch_ttinit = CH.v

                    for e in range(E):
                        for th in range(2):
                            @block.tensor
                            def _(t, tt=tt, e=e, th=th):
                                t.wait_ge(CH.h, CH.v)
                                if e >= 1 and th == 0:
                                    t.wait_ge(MA.h, MA.v)
                                CH.inc(t.matmul(PB[5][:], onehotf[:, e * 128:(e + 1) * 128],
                                                eg[:, tt * 1024 + th * 512:tt * 1024 + (th + 1) * 512],
                                                start=True, stop=True))

                            @block.vector
                            def _(v, th=th):
                                v.wait_ge(CH.h, CH.v)
                                CH.inc(v.tensor_mul(gbsb[:, th * 512:(th + 1) * 512],
                                                    PB[5][:], rb05[:, th * 512:(th + 1) * 512]))
                        ch_gate = CH.v

                        for ffh in range(2):
                            sidx = (tt * 8 + e) * 4 + ffh * 2
                            @block.gpsimd
                            def _(g, tt=tt, e=e, ffh=ffh, sidx=sidx):
                                if sidx >= 2:
                                    kind, thr = sl_done[sidx - 2]
                                    g.wait_ge((ME if kind == "h" else MX).h, thr)
                                else:
                                    g.wait_ge(CH.h, ch_moe0)
                                for kc in range(8):
                                    LW.inc(g.dma_start(
                                        out=slot[sidx % 2][:, kc * 2048:(kc + 1) * 2048],
                                        in_=w1_i[e, kc * 128:(kc + 1) * 128,
                                                 ffh * 2048:(ffh + 1) * 2048]), 16)
                            lw_w1 = LW.v
                            sl_done.append(("h", 0))

                            for ffc in range(16):
                                for th in range(2):
                                    ht = htile[0]

                                    @block.tensor
                                    def _(t, tt=tt, ffc=ffc, th=th, ht=ht,
                                          sidx=sidx, lw_w1=lw_w1):
                                        if ffc == 0 and th == 0:
                                            t.wait_ge(LW.h, lw_w1)
                                            t.wait_ge(CH.h, ch_gate)
                                        if ht >= 3:
                                            t.wait_ge(ME.h, ht - 2)
                                        for kc in range(8):
                                            ins = t.matmul(
                                                PB[ht % 3][:],
                                                slot[sidx % 2][:, kc * 2048 + ffc * 128:kc * 2048 + (ffc + 1) * 128],
                                                h2m[:, kc * 1024 + th * 512:kc * 1024 + (th + 1) * 512],
                                                start=(kc == 0), stop=(kc == 7))
                                        MH.inc(ins)

                                    @block.scalar
                                    def _(s, e=e, ffh=ffh, ffc=ffc, th=th, ht=ht):
                                        s.wait_ge(MH.h, ht + 1)
                                        ME.inc(s.activation(
                                            out=hid[:, ffc * 1024 + th * 512:ffc * 1024 + (th + 1) * 512],
                                            in_=PB[ht % 3][:], func=AF.Gelu_apprx_tanh,
                                            bias=b1mc[:, e * 32 + ffh * 16 + ffc:e * 32 + ffh * 16 + ffc + 1]))

                                    htile[0] += 1
                            sl_done[sidx] = ("h", ME.v)

                            sidx2 = sidx + 1

                            @block.gpsimd
                            def _(g, tt=tt, e=e, ffh=ffh, sidx2=sidx2):
                                if sidx2 >= 2:
                                    kind, thr = sl_done[sidx2 - 2]
                                    g.wait_ge((ME if kind == "h" else MX).h, thr)
                                else:
                                    g.wait_ge(CH.h, ch_moe0)
                                for kc2 in range(16):
                                    LW.inc(g.dma_start(
                                        out=slot[sidx2 % 2][:, kc2 * 1024:(kc2 + 1) * 1024],
                                        in_=w2_i[e, ffh * 2048 + kc2 * 128:ffh * 2048 + (kc2 + 1) * 128, :]), 16)
                            lw_w2 = LW.v
                            sl_done.append(("e", 0))

                            for dc in range(8):
                                for th in range(2):
                                    et = etile[0]

                                    @block.tensor
                                    def _(t, dc=dc, th=th, et=et, sidx2=sidx2,
                                          lw_w2=lw_w2):
                                        if dc == 0 and th == 0:
                                            t.wait_ge(LW.h, lw_w2)
                                        if et >= 2:
                                            t.wait_ge(MV.h, et - 1)
                                        for kc2 in range(16):
                                            ins = t.matmul(
                                                PB[3 + et % 2][:],
                                                slot[sidx2 % 2][:, kc2 * 1024 + dc * 128:kc2 * 1024 + (dc + 1) * 128],
                                                hid[:, kc2 * 1024 + th * 512:kc2 * 1024 + (th + 1) * 512],
                                                start=(kc2 == 0), stop=(kc2 == 15))
                                        MX.inc(ins)

                                    @block.vector
                                    def _(v, th=th, et=et):
                                        v.wait_ge(MX.h, et + 1)
                                        if et >= 2:
                                            v.wait_ge(MA.h, et - 1)
                                        MV.inc(v.tensor_mul(
                                            tev[:, (et % 2) * 512:(et % 2 + 1) * 512],
                                            PB[3 + et % 2][:],
                                            gbsb[:, th * 512:(th + 1) * 512]))

                                    @block.gpsimd
                                    def _(g, dc=dc, th=th, et=et):
                                        g.wait_ge(MV.h, et + 1)
                                        o = dc * 1024 + th * 512
                                        MA.inc(g.tensor_add(macc[:, o:o + 512],
                                                            macc[:, o:o + 512],
                                                            tev[:, (et % 2) * 512:(et % 2 + 1) * 512]))

                                    etile[0] += 1
                            sl_done[sidx2] = ("e", MX.v)

                    @block.vector
                    def _(v):
                        v.wait_ge(MA.h, MA.v)
                        CH.inc(v.engine_nop())

                    for th in range(2):
                        for dc in range(8):
                            @block.scalar
                            def _(s, dc=dc, th=th):
                                s.wait_ge(CH.h, CH.v)
                                CH.inc(s.activation(out=tev[:, 0:512],
                                                    in_=macc[:, dc * 1024 + th * 512:dc * 1024 + (th + 1) * 512],
                                                    func=AF.Square))

                            @block.tensor
                            def _(t, dc=dc, th=th):
                                t.wait_ge(CH.h, CH.v)
                                t.matmul(PB[2][0:1, :], onc_f[:],
                                         macc[:, dc * 1024 + th * 512:dc * 1024 + (th + 1) * 512],
                                         start=(dc == 0), stop=(dc == 7),
                                         skip_group_check=True)
                                CH.inc(t.matmul(PB[3][0:1, :], onc_f[:], tev[:, 0:512],
                                                start=(dc == 0), stop=(dc == 7),
                                                skip_group_check=True))

                        @block.vector
                        def _(v):
                            v.wait_ge(CH.h, CH.v)
                            CH.inc(v.tensor_scalar(out=lr_[:, 0:512], in0=PB[2][0:1, :],
                                                   scalar1=1.0 / D, scalar2=None,
                                                   op0=ALU.mult))

                        @block.scalar
                        def _(s):
                            s.wait_ge(CH.h, CH.v)
                            CH.inc(s.activation(out=lr_[:, 512:1024], in_=lr_[:, 0:512],
                                                func=AF.Square))

                        @block.vector
                        def _(v):
                            v.wait_ge(CH.h, CH.v)
                            CH.inc(v.scalar_tensor_tensor(out=lr_[:, 1024:1536],
                                                          in0=PB[3][0:1, :], scalar=1.0 / D,
                                                          in1=lr_[:, 512:1024],
                                                          op0=ALU.mult, op1=ALU.subtract))

                        @block.scalar
                        def _(s):
                            s.wait_ge(CH.h, CH.v)
                            CH.inc(s.activation(out=lr_[:, 1536:2048], in_=lr_[:, 1024:1536],
                                                func=AF.Sqrt, bias=EPS))

                        @block.vector
                        def _(v):
                            v.wait_ge(CH.h, CH.v)
                            CH.inc(v.reciprocal(lr_[:, 2048:2560], lr_[:, 1536:2048]))

                        @block.vector
                        def _(v):
                            v.wait_ge(CH.h, CH.v)
                            CH.inc(v.scalar_tensor_tensor(out=lr_[:, 2560:3072],
                                                          in0=lr_[:, 0:512], scalar=-1.0,
                                                          in1=lr_[:, 2048:2560],
                                                          op0=ALU.mult, op1=ALU.mult))

                        @block.tensor
                        def _(t):
                            t.wait_ge(CH.h, CH.v)
                            t.matmul(PB[4][:], on1_128f[:], lr_[:, 2048:2560],
                                     start=True, stop=True)
                            CH.inc(t.matmul(PB[5][:], on1_128f[:], lr_[:, 2560:3072],
                                            start=True, stop=True))

                        for dc in range(8):
                            @block.vector
                            def _(v, dc=dc, th=th):
                                v.wait_ge(CH.h, CH.v)
                                v.tensor_mul(tev[:, 0:512],
                                             macc[:, dc * 1024 + th * 512:dc * 1024 + (th + 1) * 512],
                                             PB[4][:])
                                v.tensor_add(tev[:, 512:1024], tev[:, 0:512], PB[5][:])
                                CH.inc(v.tensor_scalar(
                                    out=ffn_v()[:, dc * 512:(dc + 1) * 512],
                                    in0=tev[:, 512:1024], scalar1=lngc[:, dc:dc + 1],
                                    scalar2=lnbc[:, dc:dc + 1],
                                    op0=ALU.mult, op1=ALU.add))

                        for j in range(4):
                            tg = tt * 1024 + th * 512 + j * 128

                            @block.sync
                            def _(sy, tg=tg):
                                sy.wait_ge(CH.h, CH.v)
                                LD.inc(sy.dma_start(out=xo_v(), in_=x_own[tg:tg + 128, :]), 16)
                                LD.inc(sy.dma_start(out=tj_v(), in_=ttt_d[tg:tg + 128, :]), 16)

                            ld_oj = LD.v

                            @block.vector
                            def _(v):
                                v.wait_ge(LD.h, ld_oj)
                                v.wait_ge(CH.h, CH.v)
                                CH.inc(v.tensor_add(ot_v(), xo_v(), tj_v()))

                            for dc in range(8):
                                @block.tensor
                                def _(t, dc=dc, j=j):
                                    t.wait_ge(CH.h, CH.v)
                                    CH.inc(t.transpose(PB[6][:, 0:128],
                                                       ffn_v()[:, dc * 512 + j * 128:dc * 512 + (j + 1) * 128],
                                                       identf[:]))

                                @block.vector
                                def _(v, dc=dc):
                                    v.wait_ge(CH.h, CH.v)
                                    CH.inc(v.tensor_add(ot_v()[:, dc * 128:(dc + 1) * 128],
                                                        ot_v()[:, dc * 128:(dc + 1) * 128],
                                                        PB[6][:, 0:128]))

                            @block.sync
                            def _(sy, tg=tg):
                                sy.wait_ge(CH.h, CH.v)
                                SO.inc(sy.dma_start(out=out_o[tg:tg + 128, :],
                                                    in_=ot_v()), 16)

                @block.sync
                def _(sy):
                    sy.wait_ge(SO.h, SO.v)

            if MAXPH >= 6:
                es_moe.close()
            elif MAXPH >= 5:
                es_wo.close()
            if MAXPH >= 5:
                es_tail.close()
            if 3 <= MAXPH < 5:
                es_att.close()

    return nc


def _prep_inputs(inputs):
    f32 = np.float32
    bf = ml_dtypes.bfloat16
    x = np.asarray(inputs["L0_attn_in_seq"], f32)
    t_k = np.asarray(inputs["L0_attn_t_k"], f32)
    t_v = np.asarray(inputs["L0_attn_t_v"], f32)
    t_q = np.asarray(inputs["L0_attn_t_q"], f32)
    W1 = np.asarray(inputs["L0_attn_W1"], f32)
    b1 = np.asarray(inputs["L0_attn_b1"], f32)
    W2 = np.asarray(inputs["L0_attn_W2"], f32)
    b2 = np.asarray(inputs["L0_attn_b2"], f32)
    lng = np.asarray(inputs["L0_ffn_ln_g"], f32)
    lnb = np.asarray(inputs["L0_ffn_ln_b"], f32)
    Wq = np.asarray(inputs["L0_ffn_Wq"], f32)
    Wk = np.asarray(inputs["L0_ffn_Wk"], f32)
    Wv = np.asarray(inputs["L0_ffn_Wv"], f32)
    qsc = np.asarray(inputs["L0_ffn_q_scale"], f32).reshape(H, DH)
    ksc = np.asarray(inputs["L0_ffn_k_scale"], f32).reshape(H, DH)
    Wo = np.asarray(inputs["L0_ffn_Wo"], f32)
    gw = np.asarray(inputs["L0_ffn_gate_w"], f32)
    gb = np.asarray(inputs["L0_ffn_gate_b"], f32)
    w1m = np.asarray(inputs["L0_ffn_w1"], f32)
    b1m = np.asarray(inputs["L0_ffn_b1"], f32)
    w2m = np.asarray(inputs["L0_ffn_w2"], f32)
    b2m = np.asarray(inputs["L0_ffn_b2"], f32)

    C = np.ascontiguousarray
    com = {
        "tkT": C(t_k.T), "tvT": C(t_v.T), "tqT": C(t_q.T),
        "W1T": C(W1.T), "W2T": C(W2.T), "W2n": C(W2),
        "b1c": C(b1.reshape(2, 128).T), "b2c": C(b2.reshape(8, 128).T),
        "lng": C(lng.reshape(8, 128).T), "lnb": C(lnb.reshape(8, 128).T),
        "Wo": Wo, "gw": gw, "gb": gb.reshape(E, 1),
        "w1": w1m, "w2": w2m, "b2m": b2m,
        "b1m": C(b1m.reshape(E, 32, 128).transpose(2, 0, 1).reshape(128, E * 32)),
        "ident_f": np.eye(128, dtype=f32),
        "ident_b": np.eye(128, dtype=f32).astype(bf),
        "ones1_128f": np.ones((1, 128), f32),
        "ones_col_f": np.ones((128, 1), f32),
        "ones1_64b": np.ones((1, 64), f32).astype(bf),
        "ones8f": np.ones((8, 1), f32),
    }
    o2 = np.zeros((128, 65), f32)
    o2[0:64, 0] = 1.0
    o2[64:128, 64] = 1.0
    com["ones2w_b"] = o2.astype(bf)
    oh = np.zeros((8, E * 128), f32)
    for e in range(E):
        oh[e, e * 128:(e + 1) * 128] = 1.0
    com["onehot_f"] = oh
    cm = np.zeros((128, 4 * 512), f32)
    for i in range(4):
        kl = np.arange(128)[:, None]
        ql = np.arange(512)[None, :]
        cm[:, i * 512:(i + 1) * 512] = (i * 128 + kl <= ql).astype(f32)
    com["cmask_in"] = cm.astype(bf)

    maps = []
    for c in range(8):
        b, r = c // 4, c % 4
        m = dict(com)
        m["x_full"] = x[b]
        m["x_own"] = C(x[b, 2048 * r:2048 * (r + 1)])
        m["Wq"] = C(Wq[:, 256 * r:256 * (r + 1)])
        m["Wk"] = C(Wk[:, 256 * r:256 * (r + 1)])
        m["Wv"] = C(Wv[:, 256 * r:256 * (r + 1)])
        qs4 = qsc[4 * r:4 * r + 4]
        ks4 = ksc[4 * r:4 * r + 4]
        m["qs"] = C(np.stack([np.concatenate([qs4[2 * ch], qs4[2 * ch + 1]])
                              for ch in range(2)], axis=1))
        m["ks"] = C(np.stack([np.concatenate([ks4[2 * ch], ks4[2 * ch + 1]])
                              for ch in range(2)], axis=1))
        mk = np.zeros((128, 2), f32)
        mk[:, 0] = 1.0 if b == 0 else 0.0
        mk[:, 1] = 1.0 if b == 1 else 0.0
        m["msk"] = mk
        m["roff"] = np.array([[2048 * r]], np.int32)
        maps.append(m)
    return maps


def kernel(**inputs):
    nc = build_program()
    maps = _prep_inputs(inputs)
    res = run_bass_kernel_spmd(nc, maps, core_ids=list(range(8)))
    out = np.empty((B, S, D), np.float32)
    for c in range(8):
        b, r = c // 4, c % 4
        out[b, 2048 * r:2048 * (r + 1)] = res.results[c]["out"]
    globals()["_last_res"] = res
    return out
